# version 02
# baseline (speedup 1.0000x reference)
import numpy as np
import concourse.bass as bass
import concourse.mybir as mybir
from concourse.bass_utils import run_bass_kernel_spmd
from contextlib import ExitStack

F32 = mybir.dt.float32
BF16 = mybir.dt.bfloat16
I32 = mybir.dt.int32
AF = mybir.ActivationFunctionType
ALU = mybir.AluOpType
AX = mybir.AxisListType

NT = 32
D = 1024
H = 4096
E = 256
DK = 8
HK = 32
EPS = 1e-5
import os as _os
PHASES = _os.environ.get("KPHASES", "AB2C")
PSTAGE = int(_os.environ.get("KPSTAGE", "9"))
NORANK = _os.environ.get("KNORANK", "0") == "1"
ZROW = _os.environ.get("KZROW", "0") == "1"
DPH = _os.environ.get("KDPH", "C")
PETR = _os.environ.get("KPETR", "0") == "1"
DEBUG_OUT = _os.environ.get("KDEBUG", "0") == "1"


class Ctr:
    def __init__(self, nc, name):
        self.h = nc.semaphore(name).__enter__()
        self.n = 0

    def inc(self, instr, k=1):
        instr.then_inc(self.h, k)
        self.n += k
        return self.n

    def wait(self, eng, val=None):
        eng.wait_ge(self.h, self.n if val is None else val)


def build_nc():
    nc = bass.Bass("TRN2", target_bir_lowering=False, debug=False,
                   dynamic_dma_scratch_size=8192)

    x0d = nc.dram_tensor("x0", [NT * 128, D], F32, kind="ExternalInput")
    w1d = nc.dram_tensor("w1t", [D, H], F32, kind="ExternalInput")
    w2d = nc.dram_tensor("w2t", [H, D], F32, kind="ExternalInput")
    qpd = nc.dram_tensor("qp", [D, E], F32, kind="ExternalInput")
    kpd = nc.dram_tensor("kp", [D, E], F32, kind="ExternalInput")
    b1d = nc.dram_tensor("b1", [1, H], F32, kind="ExternalInput")
    b2d = nc.dram_tensor("b2", [1, D], F32, kind="ExternalInput")
    idbd = nc.dram_tensor("idb", [128, 128], BF16, kind="ExternalInput")
    yd = nc.dram_tensor("y", [NT * 128, D], F32, kind="ExternalOutput")
    skind = "ExternalOutput" if DEBUG_OUT else "Internal"
    x1d = nc.dram_tensor("x1s", [NT * 128, D], F32, kind=skind)
    x2d = nc.dram_tensor("x2s", [NT * 128, D], F32, kind=skind)
    if DEBUG_OUT:
        dZS = nc.dram_tensor("dZS", [128, NT], F32, kind="ExternalOutput")
        dKV = nc.dram_tensor("dKV", [128, 2048], F32, kind="ExternalOutput")
        dQT = nc.dram_tensor("dQT", [128, 8192], F32, kind="ExternalOutput")
        dXN = nc.dram_tensor("dXN", [128, 1024], F32, kind="ExternalOutput")
        dQE = nc.dram_tensor("dQE", [128, 512], F32, kind="ExternalOutput")
        dBS = nc.dram_tensor("dBS", [128, 4 * NT], F32, kind="ExternalOutput")
        dCS = nc.dram_tensor("dCS", [128, 6 * NT], F32, kind="ExternalOutput")
        dQB = nc.dram_tensor("dQB", [128, 4096], F32, kind="ExternalOutput")
        dLH = nc.dram_tensor("dLH", [3, 256], F32, kind="ExternalOutput")
        dRH = nc.dram_tensor("dRH", [3, H], F32, kind="ExternalOutput")
        dRH2 = nc.dram_tensor("dRH2", [3, D], F32, kind="ExternalOutput")
        dPS = nc.dram_tensor("dPS", [128, 1024], F32, kind="ExternalOutput")
        dW1F = nc.dram_tensor("dW1F", [128, 2 * H], F32, kind="ExternalOutput")
        dXT = nc.dram_tensor("dXT", [128, 1024], F32, kind="ExternalOutput")
    if PSTAGE == 7:
        dW1 = nc.dram_tensor("dW1", [128, DK * H], F32, kind="ExternalOutput")
        dW2 = nc.dram_tensor("dW2", [128, HK * D], F32, kind="ExternalOutput")
        dR1 = nc.dram_tensor("dR1", [3, H], F32, kind="ExternalOutput")
        dR2 = nc.dram_tensor("dR2", [3, D], F32, kind="ExternalOutput")
        dSC = nc.dram_tensor("dSC", [128, 16], F32, kind="ExternalOutput")
        dA0 = nc.dram_tensor("dA0", [128, 8 * NT], F32, kind="ExternalOutput")
        dRQ = nc.dram_tensor("dRQ", [128, 4 * NT], F32, kind="ExternalOutput")

    st = ExitStack()

    def sb(name, shape, dt):
        return st.enter_context(nc.sbuf_tensor(name, shape, dt))

    def pst_(name, shape, dt):
        return st.enter_context(nc.psum_tensor(name, shape, dt))

    w1q = sb("w1q", [128, DK * H], BF16)
    w2q = sb("w2q", [128, HK * D], BF16)
    qkp = sb("qkp", [128, 2 * DK * E], BF16)
    bfs = sb("bfs", [128, 8192], BF16)
    hbuf = sb("hbuf", [128, H], F32)
    wqb = sb("wqb", [128, 1024], F32)
    ciq = sb("ciq", [128, 1024], I32)
    cif = sb("cif", [128, 1024], F32)
    xrow = sb("xrow", [128, 2 * D], F32)
    qbf = sb("qbf", [128, 4096], BF16)
    qexp = sb("qexp", [128, 512], BF16)
    sivb = sb("sivb", [128, 8], BF16)
    rhs1 = sb("rhs1", [3, H], BF16)
    rhs2 = sb("rhs2", [3, D], BF16)
    lhs1 = sb("lhs1", [3, 256], BF16)
    lhs2 = sb("lhs2", [3, 256], BF16)
    idb = sb("idbs", [128, 128], BF16)
    onesf = sb("onesf", [1, 128], F32)
    onesb = sb("onesb", [128, 4], BF16)
    onescf = sb("onescf", [128, 4], F32)
    rq1s = sb("rq1s", [128, NT], F32)
    B1s = sb("B1s", [128, NT], F32)
    sg1s = sb("sg1s", [128, NT], F32)
    si1s = sb("si1s", [128, NT], BF16)
    m3s = sb("m3s", [128, NT], F32)
    r3s = sb("r3s", [128, NT], F32)
    zs = sb("zs", [128, NT], F32)
    wst = sb("wst", [128, 64], F32)
    a0c = sb("a0c", [128, 8 * NT], F32)
    scl = sb("scl", [128, 16], F32)
    ctmp = sb("ctmp", [128, 48], F32)
    rowt = sb("rowt", [1, 256], F32)

    csr = hbuf[0:1, 0:H]
    csr2 = qbf[0:1, 0:H]

    psA = [pst_("psA0", [128, 1024], F32), pst_("psA1", [128, 1024], F32)]
    psB = pst_("psB", [128, 1024], F32)
    psT = [pst_("psT0", [128, 1024], BF16), pst_("psT1", [128, 1024], BF16)]

    qx = Ctr(nc, "qx")
    qg = Ctr(nc, "qg")
    qr = Ctr(nc, "qr")
    cv = Ctr(nc, "cv")
    cs = Ctr(nc, "cs")
    ct = Ctr(nc, "ct")

    V, S, T, SY, G = nc.vector, nc.scalar, nc.tensor, nc.sync, nc.gpsimd

    def preload_set(*funcs):
        from concourse.hw_specs import get_activation_tables
        for idx, fs in enumerate(get_activation_tables(nc.m.arch).values()):
            if all(f in fs for f in funcs):
                return S.add_instruction(mybir.InstLoadActFuncSet(
                    name=nc.get_next_instruction_name(),
                    act_func_set_id=idx, ins=[], outs=[]))
        raise ValueError(f"no table set with {funcs}")

    def vsync(instr):
        cv.inc(instr)
        cv.wait(V)

    def barrier():
        for eng in (V, S, T, SY, G):
            for c in (qx, qg, qr, cv, cs, ct):
                c.wait(eng)

    preload_set(AF.Exp, AF.Ln, AF.Square, AF.Identity, AF.Sign, AF.Relu)

    _consts = [-EPS, EPS, 0.5, -0.5, 1e-8, 128.5]
    for _i, _cval in enumerate(_consts):
        V.memset(scl[:, 8 + _i:9 + _i], _cval)
        nc.const_aps.register(F32, _cval, scl[:, 8 + _i:9 + _i])
    V.memset(lhs1[:], -128.0)
    V.memset(lhs2[:], -128.0)
    V.memset(onesf[:], 1.0)
    V.memset(onescf[:], 1.0)
    mzi = V.memset(onesb[:], 1.0)
    cv.inc(mzi)
    cv.wait(S)
    cv.wait(T)

    qx.inc(SY.dma_start(out=idb[:], in_=idbd[:]), 16)
    for k in range(DK):
        qg.inc(G.dma_start(out=qkp[:, k * E:(k + 1) * E],
                           in_=bass.AP(qpd, k * 128 * E, [[E, 128], [1, E]])), 16)
        qg.inc(G.dma_start(out=qkp[:, 2048 + k * E: 2048 + (k + 1) * E],
                           in_=bass.AP(kpd, k * 128 * E, [[E, 128], [1, E]])), 16)
    qg.inc(G.dma_start(out=rhs1[0:1, :], in_=b1d[:]), 16)
    qg.inc(G.dma_start(out=rhs2[0:1, :], in_=b2d[:]), 16)

    if PSTAGE < 1:
        barrier()
        G.wait_ge(qg.h, qg.n)
        SY.wait_ge(qx.h, qx.n)
        st.close()
        return nc
    stat_bufs = [hbuf[:, 0:1024], hbuf[:, 1024:2048],
                 hbuf[:, 2048:3072], hbuf[:, 3072:4096]]
    stat_base = cv.n
    for i in range(64):
        buf = stat_bufs[i % 4]
        if i < 32:
            k, q = i // 4, i % 4
            src = bass.AP(w1d, (k * 128) * H + q * 1024, [[H, 128], [1, 1024]])
        else:
            j = i - 32
            src = bass.AP(w2d, (j * 128) * D, [[D, 128], [1, 1024]])
        if i >= 4:
            SY.wait_ge(cv.h, stat_base + (i - 3))
        qx.inc(SY.dma_start(out=buf, in_=src), 16)
        V.wait_ge(qx.h, qx.n)
        r = V.tensor_reduce(out=wst[:, i:i + 1], in_=buf, axis=AX.X,
                            op=ALU.add, apply_absolute_value=True)
        cv.inc(r)
    if PSTAGE < 2:
        barrier()
        G.wait_ge(qg.h, qg.n)
        SY.wait_ge(qx.h, qx.n)
        st.close()
        return nc
    cv.wait(V)
    vsync(V.tensor_reduce(out=ctmp[:, 0:1], in_=wst[:, 0:32], axis=AX.X, op=ALU.add))
    vsync(V.tensor_reduce(out=ctmp[:, 1:2], in_=wst[:, 32:64], axis=AX.X, op=ALU.add))
    cv.wait(T)
    ct.inc(T.matmul(psB[0:1, 0:2], onescf[:, 0:1], ctmp[:, 0:2], start=True, stop=True))
    ct.wait(V)
    vsync(V.tensor_copy(out=rowt[0:1, 0:2], in_=psB[0:1, 0:2]))
    vsync(V.tensor_scalar(out=rowt[0:1, 2:3], in0=rowt[0:1, 0:1],
                          scalar1=1.0 / (D * H), scalar2=None, op0=ALU.mult))
    vsync(V.tensor_scalar(out=rowt[0:1, 3:4], in0=rowt[0:1, 1:2],
                          scalar1=1.0 / (D * H), scalar2=None, op0=ALU.mult))
    vsync(V.tensor_scalar(out=rowt[0:1, 5:6], in0=rowt[0:1, 2:3],
                          scalar1=EPS, scalar2=None, op0=ALU.max))
    vsync(V.tensor_scalar(out=rowt[0:1, 7:8], in0=rowt[0:1, 3:4],
                          scalar1=EPS, scalar2=None, op0=ALU.max))
    vsync(V.reciprocal(out=rowt[0:1, 8:9], in_=rowt[0:1, 5:6]))
    vsync(V.reciprocal(out=rowt[0:1, 9:10], in_=rowt[0:1, 7:8]))
    vsync(V.tensor_scalar(out=rowt[0:1, 10:11], in0=rowt[0:1, 5:6],
                          scalar1=1.0 / 254.0, scalar2=None, op0=ALU.mult))
    vsync(V.tensor_scalar(out=rowt[0:1, 11:12], in0=rowt[0:1, 7:8],
                          scalar1=1.0 / 254.0, scalar2=None, op0=ALU.mult))
    vsync(V.tensor_scalar(out=rowt[0:1, 12:13], in0=rowt[0:1, 8:9],
                          scalar1=2.0, scalar2=None, op0=ALU.mult))
    vsync(V.tensor_scalar(out=rowt[0:1, 13:14], in0=rowt[0:1, 9:10],
                          scalar1=2.0, scalar2=None, op0=ALU.mult))
    cv.wait(T)
    for j, c in enumerate([10, 11, 12, 13, 8, 9]):
        mmb = T.matmul(psB[0:128, 4 + j:5 + j], onesf[0:1, :],
                       rowt[0:1, c:c + 1], start=True, stop=True)
    ct.inc(mmb)
    ct.wait(V)
    vsync(V.tensor_copy(out=scl[:, 0:6], in_=psB[:, 4:10]))

    if PSTAGE < 3:
        barrier()
        G.wait_ge(qg.h, qg.n)
        SY.wait_ge(qx.h, qx.n)
        st.close()
        return nc
    cv.wait(S)
    ldbufs = [wqb[:], hbuf[:, 3072:4096]]
    qbase = qx.n
    vq_base = cv.n
    for i in range(64):
        if i < 32:
            k, q = i // 4, i % 4
            src = bass.AP(w1d, (k * 128) * H + q * 1024, [[H, 128], [1, 1024]])
            dst = w1q[:, k * H + q * 1024: k * H + q * 1024 + 1024]
            isc = scl[:, 4:5]
        else:
            j = i - 32
            src = bass.AP(w2d, (j * 128) * D, [[D, 128], [1, 1024]])
            dst = w2q[:, j * D: (j + 1) * D]
            isc = scl[:, 5:6]
        buf = ldbufs[i % 2]
        if i >= 2:
            SY.wait_ge(cv.h, vq_base + (i - 1))
        qx.inc(SY.dma_start(out=buf, in_=src), 16)
        S.wait_ge(qx.h, qx.n)
        cv.wait(S)
        S.activation(out=cif[:], in_=buf, func=AF.Sign, bias=-0.5, scale=isc)
        s2i = S.activation(out=hbuf[:, 1024:2048], in_=buf, func=AF.Sign,
                           bias=0.5, scale=isc)
        cs.inc(s2i)
        cs.wait(V)
        tt = V.tensor_tensor(out=dst, in0=cif[:], in1=hbuf[:, 1024:2048], op=ALU.add)
        cv.inc(tt)

    if PSTAGE < 4:
        barrier()
        G.wait_ge(qg.h, qg.n)
        SY.wait_ge(qx.h, qx.n)
        st.close()
        return nc
    cv.wait(T)
    for sl in range(H // 512):
        for k in range(DK):
            mm = T.matmul(psA[0][0:1, (sl % 2) * 512:(sl % 2) * 512 + 512],
                          onesb[:, 0:1],
                          w1q[:, k * H + sl * 512: k * H + sl * 512 + 512],
                          start=(k == 0), stop=(k == DK - 1))
        ct.inc(mm)
        ct.wait(V)
        vsync(V.tensor_copy(out=csr[0:1, sl * 512:(sl + 1) * 512],
                            in_=psA[0][0:1, (sl % 2) * 512:(sl % 2) * 512 + 512]))
    vsync(V.tensor_copy(out=csr2[0:1, :], in_=csr[0:1, :]))
    for _c in range(4):
        vsync(V.tensor_copy(out=wqb[0:1, 0:1024],
                            in_=csr2[0:1, _c * 1024:(_c + 1) * 1024]))
        vsync(V.tensor_sub(out=csr[0:1, _c * 1024:(_c + 1) * 1024],
                           in0=csr[0:1, _c * 1024:(_c + 1) * 1024],
                           in1=wqb[0:1, 0:1024]))
    cv.wait(G)
    qg.inc(G.dma_start(out=rhs1[1:2, :], in_=csr2[0:1, :]), 16)
    qg.inc(G.dma_start(out=rhs1[2:3, :], in_=csr[0:1, :]), 16)
    G.wait_ge(qg.h, qg.n)
    qg.wait(V)
    cv.wait(T)
    for sl in range(D // 512):
        for k in range(HK):
            mm = T.matmul(psA[0][0:1, (sl % 2) * 512:(sl % 2) * 512 + 512],
                          onesb[:, 0:1],
                          w2q[:, k * D + sl * 512: k * D + sl * 512 + 512],
                          start=(k == 0), stop=(k == HK - 1))
        ct.inc(mm)
        ct.wait(V)
        vsync(V.tensor_copy(out=csr[0:1, sl * 512:(sl + 1) * 512],
                            in_=psA[0][0:1, (sl % 2) * 512:(sl % 2) * 512 + 512]))
    vsync(V.tensor_copy(out=csr2[0:1, 0:D], in_=csr[0:1, 0:D]))
    vsync(V.tensor_copy(out=wqb[0:1, 0:1024], in_=csr2[0:1, 0:D]))
    vsync(V.tensor_sub(out=csr[0:1, 0:D], in0=csr[0:1, 0:D],
                       in1=wqb[0:1, 0:1024]))
    cv.wait(G)
    qg.inc(G.dma_start(out=rhs2[1:2, :], in_=csr2[0:1, 0:D]), 16)
    qg.inc(G.dma_start(out=rhs2[2:3, :], in_=csr[0:1, 0:D]), 16)

    if PSTAGE < 5:
        barrier()
        G.wait_ge(qg.h, qg.n)
        SY.wait_ge(qx.h, qx.n)
        st.close()
        return nc
    cv.wait(S)
    for t_ in range(NT):
        xr = xrow[:, (t_ % 2) * D:(t_ % 2) * D + D]
        if t_ >= 2:
            SY.wait_ge(cs.h, cs.n)
            SY.wait_ge(cv.h, cv.n)
        qx.inc(SY.dma_start(out=xr, in_=bass.AP(x0d, t_ * 128 * D,
                                                [[D, 128], [1, D]])), 16)
        V.wait_ge(qx.h, qx.n)
        V.tensor_reduce(out=a0c[:, t_:t_ + 1], in_=xr, axis=AX.X, op=ALU.max)
        V.tensor_reduce(out=a0c[:, NT + t_:NT + t_ + 1], in_=xr,
                        axis=AX.X, op=ALU.min, negate=True)
        cv.inc(V.tensor_reduce(out=a0c[:, 2 * NT + t_:2 * NT + t_ + 1], in_=xr,
                               axis=AX.X, op=ALU.add))
        S.wait_ge(qx.h, qx.n)
        cv.wait(S)
        cs.inc(S.activation(out=cif[:], in_=xr, func=AF.Square))
        cs.wait(V)
        cv.inc(V.tensor_reduce(out=a0c[:, 3 * NT + t_:3 * NT + t_ + 1], in_=cif[:],
                               axis=AX.X, op=ALU.add))
    cs.wait(V)
    A = lambda c0, c1: a0c[:, c0 * NT:c1 * NT]
    vsync(V.tensor_scalar(out=A(4, 5), in0=A(2, 3), scalar1=1.0 / D,
                          scalar2=None, op0=ALU.mult))
    vsync(V.tensor_scalar(out=A(5, 6), in0=A(3, 4), scalar1=1.0 / D,
                          scalar2=None, op0=ALU.mult))
    vsync(V.tensor_tensor(out=A(6, 7), in0=A(4, 5), in1=A(4, 5), op=ALU.mult))
    vsync(V.tensor_sub(out=A(5, 6), in0=A(5, 6), in1=A(6, 7)))
    cv.wait(S)
    S.activation(out=A(6, 7), in_=A(5, 6), func=AF.Ln, bias=EPS)
    cs.inc(S.activation(out=A(5, 6), in_=A(6, 7), func=AF.Exp, scale=-0.5))
    cs.wait(V)
    vsync(V.tensor_sub(out=A(0, 1), in0=A(0, 1), in1=A(4, 5)))
    vsync(V.tensor_add(out=A(1, 2), in0=A(1, 2), in1=A(4, 5)))
    vsync(V.tensor_sub(out=A(6, 7), in0=A(0, 1), in1=A(1, 2)))
    vsync(V.tensor_scalar(out=A(6, 7), in0=A(6, 7), scalar1=0.0,
                          scalar2=None, op0=ALU.max))
    vsync(V.tensor_add(out=A(0, 1), in0=A(6, 7), in1=A(1, 2)))
    vsync(V.tensor_mul(out=A(0, 1), in0=A(0, 1), in1=A(5, 6)))
    vsync(V.tensor_scalar(out=A(0, 1), in0=A(0, 1), scalar1=EPS,
                          scalar2=None, op0=ALU.max))
    vsync(V.reciprocal(out=A(6, 7), in_=A(0, 1)))
    vsync(V.tensor_scalar(out=A(7, 8), in0=A(6, 7), scalar1=127.0,
                          scalar2=None, op0=ALU.mult))
    vsync(V.tensor_scalar(out=a0c[:, 6 * NT:7 * NT], in0=A(7, 8),
                          scalar1=scl[:, 2:3], scalar2=None, op0=ALU.mult))
    vsync(V.tensor_copy(out=si1s[:], in_=A(6, 7)))
    vsync(V.tensor_copy(out=si1s[:], in_=a0c[:, 6 * NT:7 * NT]))
    vsync(V.tensor_mul(out=rq1s[:], in0=A(7, 8), in1=A(5, 6)))
    vsync(V.tensor_mul(out=A(6, 7), in0=A(4, 5), in1=rq1s[:]))
    vsync(V.tensor_scalar(out=B1s[:], in0=A(6, 7), scalar1=-1.0,
                          scalar2=128.5, op0=ALU.mult, op1=ALU.add))
    vsync(V.tensor_scalar(out=sg1s[:], in0=A(0, 1), scalar1=scl[:, 0:1],
                          scalar2=None, op0=ALU.mult))

    def ffn_phase(src_d, dst_d, is_A):
        for t_ in range(NT):
            sl = t_ % 2
            xr = xrow[:, sl * D:(sl + 1) * D]
            qx.inc(SY.dma_start(out=xr, in_=bass.AP(src_d, t_ * 128 * D,
                                                     [[D, 128], [1, D]])), 16)
            ldn = qx.n

            if is_A:
                rq_c, B_c, sg_c = rq1s[:, t_:t_ + 1], B1s[:, t_:t_ + 1], sg1s[:, t_:t_ + 1]
                si_c = si1s[:, t_:t_ + 1]
            else:
                V.wait_ge(qx.h, ldn)
                m_c = m3s[:, t_:t_ + 1]
                cb = sl * 8
                cc = lambda i: ctmp[:, cb + i:cb + i + 1]
                V.tensor_reduce(out=cc(0), in_=xr, axis=AX.X, op=ALU.max)
                vsync(V.tensor_reduce(out=cc(1), in_=xr, axis=AX.X,
                                      op=ALU.min, negate=True))
                vsync(V.tensor_scalar(out=cc(0), in0=cc(0),
                                      scalar1=m_c, scalar2=None, op0=ALU.subtract))
                vsync(V.tensor_scalar(out=cc(1), in0=cc(1),
                                      scalar1=m_c, scalar2=None, op0=ALU.add))
                vsync(V.tensor_max(out=cc(4), in0=cc(0), in1=cc(1)))
                vsync(V.tensor_scalar(out=cc(4), in0=cc(4),
                                      scalar1=r3s[:, t_:t_ + 1], scalar2=EPS,
                                      op0=ALU.mult, op1=ALU.max))
                vsync(V.reciprocal(out=cc(5), in_=cc(4)))
                vsync(V.tensor_scalar(out=cc(6), in0=cc(5),
                                      scalar1=127.0, scalar2=None, op0=ALU.mult))
                vsync(V.tensor_scalar(out=cc(2), in0=cc(6),
                                      scalar1=scl[:, 2:3], scalar2=None,
                                      op0=ALU.mult))
                vsync(V.tensor_scalar(out=cc(6), in0=cc(6),
                                      scalar1=r3s[:, t_:t_ + 1], scalar2=None,
                                      op0=ALU.mult))
                vsync(V.tensor_scalar(out=cc(7), in0=m_c,
                                      scalar1=cc(6), scalar2=-1.0,
                                      op0=ALU.mult, op1=ALU.mult))
                vsync(V.tensor_scalar(out=cc(7), in0=cc(7),
                                      scalar1=128.5, scalar2=None, op0=ALU.add))
                vsync(V.tensor_scalar(out=cc(3), in0=cc(4),
                                      scalar1=scl[:, 0:1], scalar2=None,
                                      op0=ALU.mult))
                vsync(V.tensor_copy(out=sivb[:, sl:sl + 1], in_=cc(2)))
                rq_c, B_c, sg_c = cc(6), cc(7), cc(3)
                si_c = sivb[:, sl:sl + 1]
                if DEBUG_OUT:
                    V.tensor_copy(out=a0c[:, 2 * NT + t_:2 * NT + t_ + 1], in_=rq_c)
                    V.tensor_copy(out=a0c[:, 3 * NT + t_:3 * NT + t_ + 1], in_=B_c)
                    vsync(V.tensor_copy(out=a0c[:, 4 * NT + t_:4 * NT + t_ + 1], in_=sg_c))

            cs.wait(T)
            cv.wait(T)
            ct.inc(T.transpose(psT[1][0:1, 896:1024], si_c, idb[:]))
            ct.wait(V)
            if ZROW and not is_A:
                cv.inc(V.memset(lhs1[0:1, sl * 128:(sl + 1) * 128], 0.0))
            else:
                cv.inc(V.tensor_copy(out=lhs1[0:1, sl * 128:(sl + 1) * 128],
                                     in_=psT[1][0:1, 896:1024]))

            V.wait_ge(qx.h, ldn)
            cs.wait(V)
            V.tensor_scalar(out=wqb[:], in0=xr, scalar1=rq_c, scalar2=B_c,
                            op0=ALU.mult, op1=ALU.add)
            cv.inc(V.tensor_copy(out=ciq[:], in_=wqb[:]))
            cv.wait(S)
            cs.inc(S.copy(out=cif[:], in_=ciq[:]))
            cs.wait(V)
            V.tensor_tensor(out=wqb[:], in0=wqb[:], in1=cif[:], op=ALU.is_lt)
            cv.inc(V.tensor_tensor(out=qbf[:, 0:1024], in0=cif[:], in1=wqb[:],
                                   op=ALU.subtract))
            if PETR:
                cv.wait(T)
                for j in range(DK):
                    trj = T.transpose(psT[0][:, j * 128:(j + 1) * 128],
                                      qbf[:, j * 128:(j + 1) * 128], idb[:])
                ct.inc(trj)
                ct.wait(V)
                cv.inc(V.tensor_copy(out=bfs[:, 4096:5120], in_=psT[0][:]))
            else:
                cv.wait(S)
                for j in range(DK):
                    qr.inc(S.dma_start_transpose(
                        bfs[:, 4096 + j * 128: 4096 + (j + 1) * 128],
                        qbf[:, j * 128:(j + 1) * 128]), 16)

            T.wait_ge(qr.h, qr.n)
            cv.wait(T)
            cs.wait(T)
            for qq in range(4):
                bank = psA[qq % 2]
                if qq >= 2:
                    cs.wait(T)
                userank = not (NORANK and not is_A)
                for k in range(DK):
                    xqTk = bfs[:, 4096 + k * 128: 4096 + (k + 1) * 128]
                    for s2_ in range(2):
                        T.matmul(bank[:, s2_ * 512:(s2_ + 1) * 512], xqTk,
                                 w1q[:, k * H + qq * 1024 + s2_ * 512:
                                     k * H + qq * 1024 + s2_ * 512 + 512],
                                 start=(k == 0),
                                 stop=(k == DK - 1 and not userank))
                mmr = None
                for s2_ in range(2):
                    if not userank:
                        break
                    mmr = T.matmul(bank[:, s2_ * 512:(s2_ + 1) * 512],
                                   lhs1[0:3, sl * 128:(sl + 1) * 128],
                                   rhs1[0:3, qq * 1024 + s2_ * 512:
                                        qq * 1024 + s2_ * 512 + 512],
                                   start=False, stop=True)
                if mmr is None:
                    mmr = T.matmul(bank[0:1, 0:1], onesb[0:1, 0:1], onesb[0:1, 0:1],
                                   start=False, stop=True, skip_group_check=True)
                ct.inc(mmr)
                if DEBUG_OUT and (is_A != (DPH == "C")) and t_ == 0 and qq == 0:
                    ct.wait(V)
                    V.tensor_copy(out=wqb[:, 0:512], in_=bank[:, 0:512])
                    cv.inc(V.tensor_copy(out=wqb[:, 512:1024], in_=bank[:, 512:1024]))
                    cv.wait(G)
                    qg.inc(G.dma_start(out=dPS[:], in_=wqb[:]), 16)
                    qg.inc(G.dma_start(out=dXT[:], in_=bfs[:, 4096:5120]), 16)
                    G.wait_ge(qg.h, qg.n)
                    qg.wait(S)
                S.wait_ge(ct.h, ct.n)
                cs.inc(S.activation(out=hbuf[:, qq * 1024:(qq + 1) * 1024],
                                    in_=bank[:], func=AF.Gelu, scale=sg_c))

            S.wait_ge(cs.h, cs.n)
            cs.wait(V)
            hb = 16 + sl * 8
            hc = lambda i: ctmp[:, hb + i:hb + i + 1]
            vsync(V.tensor_reduce(out=hc(2), in_=hbuf[:], axis=AX.X,
                                  op=ALU.max, apply_absolute_value=True))
            vsync(V.tensor_scalar(out=hc(2), in0=hc(2),
                                  scalar1=EPS, scalar2=None, op0=ALU.max))
            if DEBUG_OUT and not is_A:
                vsync(V.tensor_copy(out=a0c[:, 5 * NT + t_:5 * NT + t_ + 1],
                                    in_=hc(2)))
            vsync(V.reciprocal(out=hc(3), in_=hc(2)))
            vsync(V.tensor_scalar(out=hc(4), in0=hc(3),
                                  scalar1=127.0, scalar2=None, op0=ALU.mult))
            vsync(V.tensor_scalar(out=hc(5), in0=hc(2),
                                  scalar1=scl[:, 1:2], scalar2=None,
                                  op0=ALU.mult))
            vsync(V.tensor_scalar(out=hc(1), in0=hc(5),
                                  scalar1=0.5, scalar2=None, op0=ALU.mult))
            vsync(V.tensor_scalar(out=hc(0), in0=hc(4),
                                  scalar1=scl[:, 3:4], scalar2=None,
                                  op0=ALU.mult))
            cv.inc(V.tensor_copy(out=sivb[:, 2 + sl:3 + sl], in_=hc(0)))
            cv.wait(V)
            cs.wait(T)
            cv.wait(T)
            ct.inc(T.transpose(psT[1][0:1, 896:1024], sivb[:, 2 + sl:3 + sl], idb[:]))
            ct.wait(V)
            cv.inc(V.tensor_copy(out=lhs2[0:1, sl * 128:(sl + 1) * 128],
                                 in_=psT[1][0:1, 896:1024]))

            for c in range(4):
                cs.wait(V)
                V.tensor_scalar(out=wqb[:], in0=hbuf[:, c * 1024:(c + 1) * 1024],
                                scalar1=hc(4), scalar2=128.5,
                                op0=ALU.mult, op1=ALU.add)
                cv.inc(V.tensor_copy(out=ciq[:], in_=wqb[:]))
                cv.wait(S)
                cs.inc(S.copy(out=cif[:], in_=ciq[:]))
                cs.wait(V)
                V.tensor_tensor(out=wqb[:], in0=wqb[:], in1=cif[:], op=ALU.is_lt)
                cv.inc(V.tensor_tensor(out=qbf[:, c * 1024:(c + 1) * 1024],
                                       in0=cif[:], in1=wqb[:], op=ALU.subtract))
                cv.wait(T)
                for j in range(8):
                    trh = T.transpose(psT[c % 2][:, j * 128:(j + 1) * 128],
                                      qbf[:, c * 1024 + j * 128:
                                          c * 1024 + (j + 1) * 128], idb[:])
                ct.inc(trh)
                ct.wait(V)
                cv.inc(V.tensor_copy(out=bfs[:, c * 1024:(c + 1) * 1024],
                                     in_=psT[c % 2][:]))

            cv.wait(T)
            for k in range(HK):
                hqTk = bfs[:, k * 128:(k + 1) * 128]
                for s2_ in range(2):
                    T.matmul(psB[:, s2_ * 512:(s2_ + 1) * 512], hqTk,
                             w2q[:, k * D + s2_ * 512: k * D + s2_ * 512 + 512],
                             start=(k == 0), stop=False)
            for s2_ in range(2):
                mmr2 = T.matmul(psB[:, s2_ * 512:(s2_ + 1) * 512],
                                lhs2[0:3, sl * 128:(sl + 1) * 128],
                                rhs2[0:3, s2_ * 512: s2_ * 512 + 512],
                                start=False, stop=True)
            ct.inc(mmr2)
            ct.wait(V)
            cv.inc(V.scalar_tensor_tensor(out=xr, in0=psB[:], scalar=hc(1),
                                          in1=xr, op0=ALU.mult, op1=ALU.add))
            cv.wait(SY)
            qx.inc(SY.dma_start(out=bass.AP(dst_d, t_ * 128 * D, [[D, 128], [1, D]]),
                                in_=xr), 16)

    if PSTAGE == 7:
        barrier()
        for kk in range(DK):
            cv.inc(V.tensor_copy(out=hbuf[:, 0:H], in_=w1q[:, kk * H:(kk + 1) * H]))
            cv.wait(G)
            qg.inc(G.dma_start(out=bass.AP(dW1, kk * H, [[DK * H, 128], [1, H]]),
                               in_=hbuf[:, 0:H]), 16)
            qg.wait(V)
        for kk in range(HK):
            cv.inc(V.tensor_copy(out=hbuf[:, 0:D], in_=w2q[:, kk * D:(kk + 1) * D]))
            cv.wait(G)
            qg.inc(G.dma_start(out=bass.AP(dW2, kk * D, [[HK * D, 128], [1, D]]),
                               in_=hbuf[:, 0:D]), 16)
            qg.wait(V)
        cv.wait(G)
        qg.inc(G.dma_start(out=dR1[0:1, :], in_=rhs1[0:1, :]), 16)
        qg.inc(G.dma_start(out=dR1[1:2, :], in_=rhs1[1:2, :]), 16)
        qg.inc(G.dma_start(out=dR1[2:3, :], in_=rhs1[2:3, :]), 16)
        qg.inc(G.dma_start(out=dR2[:], in_=rhs2[:]), 16)
        qg.inc(G.dma_start(out=dSC[:], in_=scl[:]), 16)
        qg.inc(G.dma_start(out=dA0[:], in_=a0c[:]), 16)
        qg.inc(G.dma_start(out=bass.AP(dRQ, 0, [[4 * NT, 128], [1, NT]]), in_=rq1s[:]), 16)
        qg.inc(G.dma_start(out=bass.AP(dRQ, NT, [[4 * NT, 128], [1, NT]]), in_=B1s[:]), 16)
        qg.inc(G.dma_start(out=bass.AP(dRQ, 2 * NT, [[4 * NT, 128], [1, NT]]), in_=sg1s[:]), 16)
        qg.inc(G.dma_start(out=bass.AP(dRQ, 3 * NT, [[4 * NT, 128], [1, NT]]), in_=si1s[:]), 16)
        G.wait_ge(qg.h, qg.n)
        SY.wait_ge(qx.h, qx.n)
        st.close()
        return nc

    if PSTAGE < 9:
        barrier()
        G.wait_ge(qg.h, qg.n)
        SY.wait_ge(qx.h, qx.n)
        st.close()
        return nc
    if "A" in PHASES:
        barrier()
        preload_set(AF.Gelu, AF.Square, AF.Relu, AF.Identity)
        ffn_phase(x0d, x1d, is_A=True)

    if "B" not in PHASES:
        barrier()
        G.wait_ge(qg.h, qg.n)
        SY.wait_ge(qx.h, qx.n)
        st.close()
        return nc
    barrier()
    preload_set(AF.Exp, AF.Ln, AF.Square, AF.Identity, AF.Relu)
    xn2bf = qbf[:, 2048:3072]
    xn2t = qbf[:, 3072:4096]
    for t_ in range(NT):
        sl = t_ % 2
        xr = xrow[:, sl * D:(sl + 1) * D]
        if t_ >= 2:
            SY.wait_ge(cv.h, cv.n)
        qx.inc(SY.dma_start(out=xr, in_=bass.AP(x1d, t_ * 128 * D,
                                                [[D, 128], [1, D]])), 16)
        ldn = qx.n
        S.wait_ge(qx.h, ldn)
        cv.wait(S)
        cs.inc(S.activation(out=cif[:], in_=xr, func=AF.Square))
        V.wait_ge(qx.h, ldn)
        cs.wait(V)
        V.tensor_reduce(out=ctmp[:, 0:1], in_=xr, axis=AX.X, op=ALU.add)
        cv.inc(V.tensor_reduce(out=ctmp[:, 1:2], in_=cif[:], axis=AX.X, op=ALU.add))
        vsync(V.tensor_scalar(out=ctmp[:, 2:3], in0=ctmp[:, 0:1],
                              scalar1=1.0 / D, scalar2=None, op0=ALU.mult))
        vsync(V.tensor_scalar(out=ctmp[:, 4:5], in0=ctmp[:, 1:2],
                              scalar1=1.0 / D, scalar2=None, op0=ALU.mult))
        vsync(V.tensor_mul(out=ctmp[:, 6:7], in0=ctmp[:, 2:3], in1=ctmp[:, 2:3]))
        vsync(V.tensor_sub(out=ctmp[:, 4:5], in0=ctmp[:, 4:5], in1=ctmp[:, 6:7]))
        cv.wait(S)
        S.activation(out=ctmp[:, 7:8], in_=ctmp[:, 4:5], func=AF.Ln, bias=EPS)
        cs.inc(S.activation(out=ctmp[:, 3:4], in_=ctmp[:, 7:8], func=AF.Exp,
                            scale=-0.5))
        cs.wait(V)
        vsync(V.tensor_scalar(out=ctmp[:, 5:6], in0=ctmp[:, 2:3],
                              scalar1=ctmp[:, 3:4], scalar2=-1.0,
                              op0=ALU.mult, op1=ALU.mult))
        if DEBUG_OUT:
            V.tensor_copy(out=a0c[:, t_:t_ + 1], in_=ctmp[:, 2:3])
            V.tensor_copy(out=a0c[:, NT + t_:NT + t_ + 1], in_=ctmp[:, 3:4])
            V.tensor_copy(out=a0c[:, 3 * NT + t_:3 * NT + t_ + 1], in_=ctmp[:, 1:2])
        cs.wait(V)
        V.wait_ge(qx.h, ldn)
        ct.wait(V)
        cv.inc(V.tensor_scalar(out=xn2bf, in0=xr, scalar1=ctmp[:, 3:4],
                               scalar2=ctmp[:, 5:6], op0=ALU.mult, op1=ALU.add))
        cv.wait(T)
        for j in range(8):
            trx = T.transpose(psT[0][:, j * 128:(j + 1) * 128],
                              xn2bf[:, j * 128:(j + 1) * 128], idb[:])
        ct.inc(trx)
        ct.wait(V)
        cv.inc(V.tensor_copy(out=xn2t, in_=psT[0][:]))
        cv.wait(T)
        cs.wait(T)
        for k in range(DK):
            xk = xn2t[:, k * 128:(k + 1) * 128]
            T.matmul(psB[:, 0:256], xk, qkp[:, k * E:(k + 1) * E],
                     start=(k == 0), stop=(k == DK - 1),
                     skip_group_check=True)
            mmk = T.matmul(psB[:, 256:512], xk,
                           qkp[:, 2048 + k * E: 2048 + (k + 1) * E],
                           start=False, stop=(k == DK - 1),
                           skip_group_check=True)
        ct.inc(mmk)
        ct.wait(S)
        S.activation(out=qexp[:, 0:256], in_=psB[:, 0:256], func=AF.Exp)
        cs.inc(S.activation(out=qexp[:, 256:512], in_=psB[:, 256:512], func=AF.Exp))
        cs.wait(V)
        vsync(V.tensor_reduce(out=ctmp[:, 6:7], in_=qexp[:, 0:256],
                              axis=AX.X, op=ALU.add))
        vsync(V.tensor_scalar(out=ctmp[:, 7:8], in0=ctmp[:, 6:7],
                              scalar1=1e-8, scalar2=None, op0=ALU.add))
        cv.inc(V.reciprocal(out=zs[:, t_:t_ + 1], in_=ctmp[:, 7:8]))
        if DEBUG_OUT:
            S.copy(out=a0c[:, 2 * NT + t_:2 * NT + t_ + 1], in_=ctmp[:, 6:7])
        cs.wait(T)
        for e in range(2):
            trq = T.transpose(psT[1][:, e * 128:(e + 1) * 128],
                              qexp[:, e * 128:(e + 1) * 128], idb[:])
        ct.inc(trq)
        ct.wait(V)
        for e in range(2):
            evq = V.tensor_copy(out=bfs[:, e * 4096 + t_ * 128:
                                        e * 4096 + (t_ + 1) * 128],
                                in_=psT[1][:, e * 128:(e + 1) * 128])
        cv.inc(evq)
        cv.wait(T)
        for e in range(2):
            for s2_ in range(2):
                mkv = T.matmul(psA[e][:, s2_ * 512:(s2_ + 1) * 512],
                               qexp[:, 256 + e * 128: 256 + (e + 1) * 128],
                               xn2bf[:, s2_ * 512:(s2_ + 1) * 512],
                               start=(t_ == 0), stop=(t_ == NT - 1),
                               skip_group_check=True)
        if t_ == NT - 1:
            ct.inc(mkv)
    ct.wait(V)
    for e in range(2):
        kve = V.tensor_copy(out=qbf[:, e * 1024:(e + 1) * 1024], in_=psA[e][:])
    cv.inc(kve)
    if DEBUG_OUT:
        cv.wait(SY)
        qx.inc(SY.dma_start(out=dZS[:], in_=zs[:]), 16)
        qg.inc(G.dma_start(out=dKV[:], in_=qbf[:, 0:2048]), 16)
        qg.inc(G.dma_start(out=dQT[:], in_=bfs[:]), 16)
        qg.inc(G.dma_start(out=dXN[:], in_=qbf[:, 2048:3072]), 16)
        qg.inc(G.dma_start(out=dQE[:], in_=qexp[:]), 16)
        cs.wait(SY)
        qx.inc(SY.dma_start(out=dBS[:], in_=a0c[:, 0:4 * NT]), 16)
        G.wait_ge(qg.h, qg.n)
        SY.wait_ge(qx.h, qx.n)
        qg.wait(V)
        qx.wait(V)

    if "2" not in PHASES:
        barrier()
        G.wait_ge(qg.h, qg.n)
        SY.wait_ge(qx.h, qx.n)
        st.close()
        return nc
    barrier()
    for t_ in range(NT):
        sl = t_ % 2
        xr = xrow[:, sl * D:(sl + 1) * D]
        qx.inc(SY.dma_start(out=xr, in_=bass.AP(x1d, t_ * 128 * D,
                                                [[D, 128], [1, D]])), 16)
        ldn = qx.n
        cv.wait(T)
        for e in range(2):
            for s2_ in range(2):
                mnm = T.matmul(psB[:, s2_ * 512:(s2_ + 1) * 512],
                               bfs[:, e * 4096 + t_ * 128: e * 4096 + (t_ + 1) * 128],
                               qbf[:, e * 1024 + s2_ * 512:
                                   e * 1024 + s2_ * 512 + 512],
                               start=(e == 0), stop=(e == 1))
        ct.inc(mnm)
        ct.wait(V)
        V.wait_ge(qx.h, ldn)
        cv.inc(V.scalar_tensor_tensor(out=xr, in0=psB[:], scalar=zs[:, t_:t_ + 1],
                                      in1=xr, op0=ALU.mult, op1=ALU.add))
        cv.wait(SY)
        qx.inc(SY.dma_start(out=bass.AP(x2d, t_ * 128 * D, [[D, 128], [1, D]]),
                            in_=xr), 16)
        cv.wait(S)
        cs.inc(S.activation(out=cif[:], in_=xr, func=AF.Square))
        cs.wait(V)
        V.tensor_reduce(out=ctmp[:, 0:1], in_=xr, axis=AX.X, op=ALU.add)
        cv.inc(V.tensor_reduce(out=ctmp[:, 1:2], in_=cif[:], axis=AX.X, op=ALU.add))
        vsync(V.tensor_scalar(out=m3s[:, t_:t_ + 1], in0=ctmp[:, 0:1],
                              scalar1=1.0 / D, scalar2=None, op0=ALU.mult))
        vsync(V.tensor_scalar(out=ctmp[:, 3:4], in0=ctmp[:, 1:2],
                              scalar1=1.0 / D, scalar2=None, op0=ALU.mult))
        vsync(V.tensor_mul(out=ctmp[:, 4:5], in0=m3s[:, t_:t_ + 1],
                           in1=m3s[:, t_:t_ + 1]))
        vsync(V.tensor_sub(out=ctmp[:, 3:4], in0=ctmp[:, 3:4], in1=ctmp[:, 4:5]))
        cv.wait(S)
        S.activation(out=ctmp[:, 4:5], in_=ctmp[:, 3:4], func=AF.Ln, bias=EPS)
        cs.inc(S.activation(out=r3s[:, t_:t_ + 1], in_=ctmp[:, 4:5],
                            func=AF.Exp, scale=-0.5))

    if _os.environ.get("KPSAN", "0") == "1":
        barrier()
        V.memset(psA[0][:], 0.0)
        V.memset(psA[1][:], 0.0)
        cv.inc(V.memset(psB[:], 0.0))
        cv.wait(T)

    if "C" not in PHASES:
        barrier()
        G.wait_ge(qg.h, qg.n)
        SY.wait_ge(qx.h, qx.n)
        st.close()
        return nc
    barrier()
    preload_set(AF.Gelu, AF.Square, AF.Relu, AF.Identity)
    ffn_phase(x2d, yd, is_A=False)

    if DEBUG_OUT:
        barrier()
        cv.wait(G)
        V.tensor_copy(out=a0c[:, 0:NT], in_=m3s[:])
        vsync(V.tensor_copy(out=a0c[:, NT:2 * NT], in_=r3s[:]))
        cv.wait(G)
        qg.inc(G.dma_start(out=dCS[:], in_=a0c[:, 0:6 * NT]), 16)
        qg.inc(G.dma_start(out=dQB[:], in_=qbf[:]), 16)
        qg.inc(G.dma_start(out=dLH[:], in_=lhs1[:]), 16)
        qg.inc(G.dma_start(out=dRH[:], in_=rhs1[:]), 16)
        cv.wait(G)
        cv.inc(V.tensor_copy(out=hbuf[:], in_=w1q[:, 0:H]))
        cv.wait(G)
        qg.inc(G.dma_start(out=bass.AP(dW1F, 0, [[2 * H, 128], [1, H]]), in_=hbuf[:]), 16)
        qg.wait(V)
        cv.inc(V.tensor_copy(out=hbuf[:], in_=w1q[:, H:2 * H]))
        cv.wait(G)
        qg.inc(G.dma_start(out=bass.AP(dW1F, H, [[2 * H, 128], [1, H]]), in_=hbuf[:]), 16)
        qg.wait(V)
        qg.inc(G.dma_start(out=dRH2[:], in_=rhs2[:]), 16)
    G.wait_ge(qg.h, qg.n)
    SY.wait_ge(qx.h, qx.n)

    st.close()
    return nc


_nc_cache = None
_last_res = None


def kernel(L0_attn_x, L0_attn_q_proj, L0_attn_k_proj, L0_ffn_W1, L0_ffn_b1,
           L0_ffn_W2, L0_ffn_b2):
    import ml_dtypes
    global _nc_cache
    x = np.ascontiguousarray(np.asarray(L0_attn_x, dtype=np.float32))
    w1t = np.ascontiguousarray(np.asarray(L0_ffn_W1, dtype=np.float32).T)
    w2t = np.ascontiguousarray(np.asarray(L0_ffn_W2, dtype=np.float32).T)
    qp = np.ascontiguousarray(np.asarray(L0_attn_q_proj, dtype=np.float32))
    kp = np.ascontiguousarray(np.asarray(L0_attn_k_proj, dtype=np.float32))
    b1 = np.ascontiguousarray(np.asarray(L0_ffn_b1, dtype=np.float32).reshape(1, -1))
    b2 = np.ascontiguousarray(np.asarray(L0_ffn_b2, dtype=np.float32).reshape(1, -1))
    idb = np.eye(128, dtype=ml_dtypes.bfloat16)

    if _nc_cache is None:
        _nc_cache = build_nc()
    nc = _nc_cache

    in_maps = [{"x0": np.ascontiguousarray(x[b]), "w1t": w1t, "w2t": w2t,
                "qp": qp, "kp": kp, "b1": b1, "b2": b2, "idb": idb}
               for b in range(8)]
    res = run_bass_kernel_spmd(nc, in_maps, core_ids=list(range(8)))
    global _last_res
    _last_res = res
    return np.stack([res.results[b]["y"] for b in range(8)], axis=0)


if __name__ == "__main__":
    import importlib.util
    spec = importlib.util.spec_from_file_location("ref", "/root/problem/reference.py")
    ref = importlib.util.module_from_spec(spec)
    spec.loader.exec_module(ref)
    inputs = {k: np.asarray(v) for k, v in ref.setup_inputs(0).items()}
    out = kernel(**inputs)
    exp = np.asarray(ref.reference(**inputs))
    err = np.linalg.norm(out - exp) / np.linalg.norm(exp)
    print("rel_err:", err)
